# version 04
# baseline (speedup 1.0000x reference)
import numpy as np
import ml_dtypes
from contextlib import ExitStack

import concourse.bass as bass
import concourse.mybir as mybir
from concourse.bass_utils import run_bass_kernel_spmd

F32 = mybir.dt.float32
BF16 = mybir.dt.bfloat16
AF = mybir.ActivationFunctionType
ALU = mybir.AluOpType
AX = mybir.AxisListType

P = 128
T = 4096
D = 1024
I = 2816
NT = T // P
ND = D // P
NI = I // P
H = 16
DH = 64
QT = 512
NQT = T // QT
KC = 128
NKC = T // KC
EPS = 1e-5

MID = [0.25, 0.75, 1.25, 1.75, 2.5, 3.5, 5.0]

_STAGE = "full"
_DEBUG = False


class Sem:
    def __init__(self, handle):
        self.h = handle
        self.n = 0

    def inc(self, k=1):
        self.n += k
        return self.n


def build_nc():
    nc = bass.Bass("TRN2", target_bir_lowering=False)

    x_in = nc.dram_tensor("x", [T, D], F32, kind="ExternalInput").ap()
    wg_in = nc.dram_tensor("wg", [I, D], F32, kind="ExternalInput").ap()
    wu_in = nc.dram_tensor("wu", [I, D], F32, kind="ExternalInput").ap()
    wd_in = nc.dram_tensor("wd", [D, I], F32, kind="ExternalInput").ap()
    wT_in = {n: nc.dram_tensor(n, [D, D], F32, kind="ExternalInput").ap()
             for n in ("wqT", "wkT", "wvT", "woT")}
    b_in = {n: nc.dram_tensor(n, [P, ND], F32, kind="ExternalInput").ap()
            for n in ("bq_r", "bk_r", "bv_r")}
    bo_in = nc.dram_tensor("bo", [1, D], F32, kind="ExternalInput").ap()
    ident_in = nc.dram_tensor("ident", [P, P], F32, kind="ExternalInput").ap()
    ones_in = nc.dram_tensor("onesr", [1, P], F32, kind="ExternalInput").ap()
    ind_in = nc.dram_tensor("ind", [ND * P, H], F32, kind="ExternalInput").ap()
    out_hbm = nc.dram_tensor("out", [T, D], F32, kind="ExternalOutput").ap()

    dbg = {}
    if _DEBUG:
        for nm, shape, dt in [("dbg_statm", [P, NT], F32), ("dbg_statr", [P, NT], F32),
                              ("dbg_xln", [P, D], F32), ("dbg_xq", [P, D], F32),
                              ("dbg_xqT", [P, ND * P], F32), ("dbg_sg", [P, 512], F32),
                              ("dbg_h", [P, 512], F32), ("dbg_hq", [P, 512], F32),
                              ("dbg_wgT", [P, I], F32), ("dbg_var", [P, NT], F32),
                              ("dbg_lnvar", [P, NT], F32), ("dbg_ssq", [P, NT], F32),
                              ("dbg_K", [P, 512], F32), ("dbg_V", [P, 1040], F32),
                              ("dbg_Qt", [P, ND * QT], F32), ("dbg_negc", [P, H], F32),
                              ("dbg_O", [P, ND * QT], F32), ("dbg_kmax", [P, 4], F32),
                              ("dbg_xlc", [P, ND * QT], F32), ("dbg_wproj", [P, D], F32),
                              ("dbg_K7", [P, 512], F32), ("dbg_V7", [P, 1040], F32)]:
            dbg[nm] = nc.dram_tensor(nm, shape, dt, kind="ExternalOutput").ap()

    x1_hbm = nc.dram_tensor("x1s", [T, D], F32, kind="Internal").ap()
    x2_hbm = nc.dram_tensor("x2s", [T, D], F32, kind="Internal").ap()
    q_hbm = nc.dram_tensor("qs", [D, T], BF16, kind="Internal").ap()
    wgT_hbm = nc.dram_tensor("wgTs", [D, I], BF16, kind="Internal").ap()
    wuT_hbm = nc.dram_tensor("wuTs", [D, I], BF16, kind="Internal").ap()
    wdT_hbm = nc.dram_tensor("wdTs", [I, D], BF16, kind="Internal").ap()

    st = ExitStack()
    sbt = lambda name, shape, dt=F32: st.enter_context(nc.sbuf_tensor(name, shape, dt))

    s_ldx = Sem(st.enter_context(nc.semaphore("ldx")))
    s_ldw = Sem(st.enter_context(nc.semaphore("ldw")))
    s_pe = Sem(st.enter_context(nc.semaphore("pe")))
    s_dv = Sem(st.enter_context(nc.semaphore("dv")))
    s_ac = Sem(st.enter_context(nc.semaphore("ac")))
    s_sto = Sem(st.enter_context(nc.semaphore("sto")))

    streams = {k: [] for k in ("sync", "gpsimd", "tensor", "vector", "scalar")}

    def on(engine, fn):
        streams[engine].append(fn)

    def wait(engine, sem, val):
        if val is None:
            return
        on(engine, lambda e: e.wait_ge(sem.h, val))

    def dma(engine, out, in_, sem):
        on(engine, lambda e: e.dma_start(out=out, in_=in_).then_inc(sem.h, 16))
        return sem.inc(16)

    def op(engine, fn, sem=None):
        if sem is None:
            on(engine, fn)
            return None
        on(engine, lambda e: fn(e).then_inc(sem.h, 1))
        return sem.inc()

    PS = st.enter_context(nc.psum_tensor("PS", [P, 4096], F32))

    def bank_f32(b, cols=512, rows=P, row0=0):
        return PS[row0:row0 + rows, b * 512: b * 512 + cols]

    def bank_bf16(b, cols, rows=P, row0=0, off=0):
        v = PS[row0:row0 + rows, b * 512:(b + 1) * 512].bitcast(BF16)
        return v[:, off:off + cols]

    ident_bf = sbt("ident_bf", [P, P], BF16)
    ident_f32 = sbt("ident_f32", [P, P], F32)
    ones_bf = sbt("ones_bf", [1, P], BF16)
    b_sb = {n: sbt(n + "_sb", [P, ND], F32) for n in ("bq_r", "bk_r", "bv_r")}
    ind_sb = sbt("ind_sb", [P, ND * H], BF16)

    AR_N = 86784
    AR = sbt("AR", [P, AR_N], BF16)

    def ar(lo, n, dt=BF16):
        v = AR[:, lo:lo + n]
        return v if dt == BF16 else v.bitcast(dt)

    wgT_sb = ar(0, ND * I)
    wuT_sb = ar(ND * I, ND * I)
    wdT_sb = ar(2 * ND * I, NI * D)
    _q0 = 3 * ND * I
    q_y = ar(_q0, I)
    q_a = [ar(_q0 + I, I), ar(_q0 + 2 * I, I)]
    q_sg = ar(_q0 + 3 * I, I)
    qf_y = q_y.bitcast(F32)
    qf_a = [q_a[0].bitcast(F32), q_a[1].bitcast(F32)]
    qf_sg = q_sg.bitcast(F32)

    K_sb = ar(0, ND * T)
    V2T_sb = ar(ND * T, NT * H * 65)
    wproj_sb = ar(66048, ND * D * 2)
    xlc_sb = [ar(74240, ND * QT), ar(78336, ND * QT)]
    woT_sb = ar(66048, ND * D * 2)
    Qt_sb = ar(74240, ND * QT)
    OallT_sb = ar(78336, ND * QT)
    Ostg_sb = ar(82432, ND * QT)

    xin_sb = [sbt(f"xin{i}", [P, D], F32) for i in range(2)]
    x1t_sb = [sbt(f"x1t{i}", [P, D], F32) for i in range(2)]
    sq_scr = x1t_sb[0][:]
    stat_m = sbt("stat_m", [P, NT])
    stat_r = sbt("stat_r", [P, NT])
    xln_sb = sbt("xln_sb", [P, D], F32)
    bo_bc = xln_sb[:]
    q_bm = sbt("q_bm", [P, I // 16])
    q_sc = sbt("q_sc", [P, I // 16])
    q_rs = sbt("q_rs", [P, I // 16])
    ssq_sb = q_rs[:, 40:40 + NT]
    ssm_sb = q_rs[:, 72:72 + NT]
    lnv = [q_bm[:, 40 + i * NT:40 + (i + 1) * NT] for i in range(3)]
    xq_sb = [sbt(f"xq{i}", [P, D], BF16) for i in range(2)]
    xqT_sb = [sbt(f"xqT{i}", [P, ND * P], BF16) for i in range(2)]
    sg_sb = sbt("sg_sb", [P, 512], F32)
    h_sb = sbt("h_sb", [P, 512], F32)
    hq_sb = [sbt(f"hq{i}", [P, 512], BF16) for i in range(2)]
    hqT_sb = [ar(_q0 + 4 * I, NI * P), ar(_q0 + 4 * I + NI * P, NI * P)]
    wtmp_sb = [h for h in hqT_sb]
    sums_sb = h_sb[:]
    rrow_sb = sg_sb.bitcast(BF16)[:, 0:512]
    rb_sb = xq_sb[0].bitcast(F32)[:, 0:QT]
    ksq_sb = h_sb.bitcast(BF16)[:, 0:512]
    Pch_sb = [xqT_sb[0], xqT_sb[1]]
    srt_sb = q_bm[:, 0:8]
    kmax_sb = q_bm[:, 8:12]
    qmax_sb = q_bm[:, 12:16]
    tmax_sb = q_bm[:, 16:20]
    negc_sb = q_sc[:, 0:H]
    cc_sb = q_sc[:, H:2 * H]

    def quant(src_bf, dst_bf, n, sem=None):
        assert n <= 1408
        nb = n // 16
        y = qf_y[:, 0:n]
        a0, a1 = qf_a[0][:, 0:n], qf_a[1][:, 0:n]
        sg = qf_sg[:, 0:n]
        bm, sc, rs = q_bm[:, 0:nb], q_sc[:, 0:nb], q_rs[:, 0:nb]
        s3 = lambda tns: tns.rearrange("p (b s) -> p b s", s=16)
        bcast = lambda tns: tns.rearrange("p b -> p b ()").broadcast_to([tns.shape[0], nb, 16])
        f1 = s_dv.inc()
        f2 = s_dv.inc()
        last_v = sem.inc() if sem is not None else None

        def f(v):
            v.reduce_max(bm, s3(src_bf), axis=AX.X, apply_absolute_value=True)
            v.tensor_single_scalar(a0, src_bf, -1.0, ALU.mult)
            v.tensor_scalar(sc, bm, 1.0 / 6.0, 1e-12, ALU.mult, ALU.max)
            v.tensor_tensor(a1, a0, src_bf, ALU.max)
            v.reciprocal(rs, sc).then_inc(s_dv.h, 1)
            v.wait_ge(s_dv.h, f1)
            v.tensor_tensor(s3(y), s3(a1), bcast(rs), ALU.mult)
            v.tensor_scalar(sg, src_bf, 3e38, 1.0, ALU.mult, ALU.min)
            v.tensor_single_scalar(a0, y, MID[0], ALU.is_gt)
            v.tensor_scalar(a1, y, MID[6], 2.0, ALU.is_gt, ALU.mult)
            v.scalar_tensor_tensor(a0, y, MID[1], a0, ALU.is_gt, ALU.add)
            v.scalar_tensor_tensor(a1, y, MID[4], a1, ALU.is_gt, ALU.add)
            v.scalar_tensor_tensor(a0, y, MID[2], a0, ALU.is_gt, ALU.add)
            v.tensor_single_scalar(sg, sg, -1.0, ALU.max)
            v.scalar_tensor_tensor(a0, y, MID[3], a0, ALU.is_gt, ALU.add)
            v.scalar_tensor_tensor(a1, y, MID[5], a1, ALU.is_gt, ALU.add)
            v.tensor_tensor(s3(sg), s3(sg), bcast(sc), ALU.mult)
            v.scalar_tensor_tensor(a1, a0, 0.5, a1, ALU.mult, ALU.add).then_inc(s_dv.h, 1)
            v.wait_ge(s_dv.h, f2)
            last = v.tensor_tensor(dst_bf, a1, sg, ALU.mult)
            if sem is not None:
                last.then_inc(sem.h, 1)

        on("vector", f)
        return last_v

    def ln_stats_phase(src_hbm, mdst, rdst, pre_wait_sync=()):
        src_t = src_hbm.rearrange("(t p) d -> t p d", p=P)
        ac_list, dv_list = [], []
        for (sem, val) in pre_wait_sync:
            wait("sync", sem, val)
        for t in range(NT):
            if t >= 2:
                wait("sync", s_ac, ac_list[t - 2])
                wait("sync", s_dv, dv_list[t - 2])
            lv = dma("sync", xin_sb[t % 2][:], src_t[t], s_ldx)
            wait("scalar", s_ldx, lv)
            ac_list.append(op("scalar", lambda s, t=t: s.activation(
                sq_scr[:], xin_sb[t % 2][:], AF.Square,
                accum_out=ssq_sb[:, t:t + 1]), s_ac))
            wait("vector", s_ldx, lv)
            dv_list.append(op("vector", lambda v, t=t: v.reduce_sum(
                ssm_sb[:, t:t + 1], xin_sb[t % 2][:], axis=AX.X), s_dv))
        wait("vector", s_ac, ac_list[-1])
        on("vector", lambda v: v.tensor_scalar(mdst[:], ssm_sb[:], 1.0 / D, None, ALU.mult))
        on("vector", lambda v: v.tensor_scalar(lnv[1][:], ssq_sb[:], 1.0 / D, None, ALU.mult))
        fen = op("vector", lambda v: v.tensor_tensor(lnv[0][:], mdst[:], mdst[:], ALU.mult), s_dv)
        wait("vector", s_dv, fen)
        dvt = op("vector", lambda v: v.scalar_tensor_tensor(
            lnv[2][:], lnv[1][:], EPS, lnv[0][:], ALU.add, ALU.subtract), s_dv)
        wait("scalar", s_dv, dvt)
        on("scalar", lambda s: s.activation(lnv[1][:], lnv[2][:], AF.Ln))
        done = op("scalar", lambda s: s.activation(rdst[:], lnv[1][:], AF.Exp, scale=-0.5), s_ac)
        if _DEBUG and dbg and "dbg_var" in dbg and not dbg.get("_var_used"):
            dbg["_var_used"] = True
            wait("gpsimd", s_ac, done)
            dma("gpsimd", dbg["dbg_var"], lnv[2][:], s_sto)
            dma("gpsimd", dbg["dbg_lnvar"], lnv[1][:], s_sto)
            dma("gpsimd", dbg["dbg_ssq"], ssq_sb[:], s_sto)
        return done, ac_list[-1], dv_list[-1]

    def weight_prep(pre_wait_vec=None):
        if pre_wait_vec is not None:
            wait("vector", s_ac, pre_wait_vec)
        wtmpf = [hqT_sb[0].bitcast(F32), hqT_sb[1].bitcast(F32)]
        wqd = x1t_sb[1].bitcast(BF16)[:, 0:1408]
        pe_by_buf = [None, None]
        wqd_pe = None
        evacs = [None]
        jobs = [(wg_in, NI, D, wgT_sb, wgT_hbm, I),
                (wu_in, NI, D, wuT_sb, wuT_hbm, I),
                (wd_in, ND, I, wdT_sb, wdT_hbm, D)]
        k = 0
        for (src_, ntile, width, dstT, dsthbm, tdim) in jobs:
            nhalf = (width + 1407) // 1408
            src_t = src_.rearrange("(t p) d -> t p d", p=P)
            for t in range(ntile):
                for half in range(nhalf):
                    w0 = half * 1408
                    wl_ = min(1408, width - w0)
                    b = k % 2
                    k += 1
                    buf = wtmpf[b][:, 0:wl_]
                    wait("gpsimd", s_dv, pe_by_buf[b])
                    lv = dma("gpsimd", buf, src_t[t, :, w0:w0 + wl_], s_ldw)
                    wait("vector", s_ldw, lv)
                    wait("vector", s_pe, wqd_pe)
                    dvq = quant(buf, wqd[:, 0:wl_], wl_, sem=s_dv)
                    pe_by_buf[b] = dvq
                    wait("tensor", s_dv, dvq)
                    nchunk = wl_ // P
                    pe_last = None
                    for c in range(nchunk):
                        pe_last = op("tensor", lambda te, c=c: te.transpose(
                            bank_bf16(c // 8, P, off=(c % 8) * P),
                            wqd[:, c * P:(c + 1) * P], ident_bf[:]),
                            s_pe if c == nchunk - 1 else None)
                    wqd_pe = pe_last
                    wait("vector", s_pe, pe_last)
                    ev = None
                    for c in range(nchunk):
                        gc = w0 // P + c
                        ev = op("vector", lambda v, c=c, gc=gc, t=t, dstT=dstT, tdim=tdim: v.tensor_copy(
                            dstT[:, gc * tdim + t * P: gc * tdim + (t + 1) * P],
                            bank_bf16(c // 8, P, off=(c % 8) * P)), s_dv)
                    evacs.append(ev)
            wait("sync", s_dv, evacs[-1])
            nchunk_t = width // P
            for c in range(nchunk_t):
                dma("sync", dsthbm.rearrange("(c p) t -> c p t", p=P)[c],
                    dstT[:, c * tdim:(c + 1) * tdim], s_sto)
        return evacs[-1]

    FSL = [(i * 512, 512) for i in range(I // 512)]
    if I % 512:
        FSL.append((I - I % 512, I % 512))

    def ffn_phase(src_hbm, mstat, rstat, dst_hbm, w_ready_dv=None, w_ready_gp=None,
                  pre_wait=(), load_weights=False):
        if load_weights:
            wl = None
            for (dstT, srch, tdim, nchunk) in [
                    (wgT_sb, wgT_hbm, I, ND), (wuT_sb, wuT_hbm, I, ND),
                    (wdT_sb, wdT_hbm, D, NI)]:
                for c in range(nchunk):
                    wl = dma("gpsimd", dstT[:, c * tdim:(c + 1) * tdim],
                             srch.rearrange("(c p) t -> c p t", p=P)[c], s_ldw)
            w_ready_gp = wl
        src_t = src_hbm.rearrange("(t p) d -> t p d", p=P)
        dst_t = dst_hbm.rearrange("(t p) d -> t p d", p=P)
        resid = [None, None]
        sto_hist = [None, None]
        g_free = {0: None, 1: None}
        u_free = {0: None, 1: None}
        z_free = {0: None, 1: None}
        tr2_free = None
        tr0_free = None
        first_mm_waited = False
        for (eng, sem, val) in pre_wait:
            wait(eng, sem, val)
        for t in range(NT):
            xb = xin_sb[t % 2]
            wait("sync", s_dv, resid[t % 2])
            lv = dma("sync", xb[:], src_t[t], s_ldx)
            wait("vector", s_ldx, lv)
            xq = xq_sb[t % 2]
            xqT = xqT_sb[t % 2]
            on("vector", lambda v, t=t, xb=xb: v.tensor_scalar(
                xln_sb[:], xb[:], mstat[:, t:t + 1], rstat[:, t:t + 1],
                ALU.subtract, ALU.mult))
            wait("vector", s_pe, tr2_free)
            dvq = quant(xln_sb[:], xq[:], D, sem=s_dv)
            wait("tensor", s_dv, dvq)
            if _DEBUG and t == 0 and dbg:
                wait("gpsimd", s_dv, dvq)
                dma("gpsimd", dbg["dbg_xln"], xln_sb[:], s_sto)
                dma("gpsimd", dbg["dbg_xq"], xq[:], s_sto)
                dma("gpsimd", dbg["dbg_statm"], stat_m[:], s_sto)
                dma("gpsimd", dbg["dbg_statr"], stat_r[:], s_sto)
                dma("gpsimd", dbg["dbg_wgT"], wgT_sb[:, 0:I], s_sto)
            if not first_mm_waited:
                if w_ready_dv is not None:
                    wait("tensor", s_dv, w_ready_dv)
                if w_ready_gp is not None:
                    wait("tensor", s_ldw, w_ready_gp)
                first_mm_waited = True
            pe_t = None
            for c in range(ND):
                pe_t = op("tensor", lambda te, c=c, xq=xq: te.transpose(
                    bank_bf16(1, P, off=c * P), xq[:, c * P:(c + 1) * P],
                    ident_bf[:]), s_pe if c == ND - 1 else None)
            tr2_free = pe_t
            wait("vector", s_pe, pe_t)
            ev = None
            for c in range(ND):
                ev = op("vector", lambda v, c=c, xqT=xqT: v.tensor_copy(
                    xqT[:, c * P:(c + 1) * P], bank_bf16(1, P, off=c * P)), s_dv)
            if _DEBUG and t == 0 and dbg:
                wait("gpsimd", s_dv, ev)
                dma("gpsimd", dbg["dbg_xqT"], xqT[:], s_sto)
            hqT = hqT_sb[t % 2]
            last_dvh = None
            z_done = [None, None]
            for si, (f0, fl) in enumerate(FSL):
                slot = si % 2
                gb, ub = 2 + slot, 4 + slot
                wait("tensor", s_ac, g_free[slot])
                wait("tensor", s_dv, u_free[slot])
                if si == 0:
                    wait("tensor", s_dv, ev)
                g_ready = None
                for c in range(ND):
                    g_ready = op("tensor", lambda te, c=c, f0=f0, fl=fl, gb=gb, xqT=xqT: te.matmul(
                        bank_f32(gb, fl), xqT[:, c * P:(c + 1) * P],
                        wgT_sb[:, c * I + f0: c * I + f0 + fl],
                        start=(c == 0), stop=(c == ND - 1)),
                        s_pe if c == ND - 1 else None)
                u_ready = None
                for c in range(ND):
                    u_ready = op("tensor", lambda te, c=c, f0=f0, fl=fl, ub=ub, xqT=xqT: te.matmul(
                        bank_f32(ub, fl), xqT[:, c * P:(c + 1) * P],
                        wuT_sb[:, c * I + f0: c * I + f0 + fl],
                        start=(c == 0), stop=(c == ND - 1)),
                        s_pe if c == ND - 1 else None)
                wait("scalar", s_pe, g_ready)
                wait("scalar", s_dv, last_dvh)
                ac_s = op("scalar", lambda s, fl=fl, gb=gb: s.activation(
                    sg_sb[:, 0:fl], bank_f32(gb, fl), AF.Silu), s_ac)
                g_free[slot] = ac_s
                wait("vector", s_ac, ac_s)
                wait("vector", s_pe, u_ready)
                dv_h = op("vector", lambda v, fl=fl, ub=ub: v.tensor_tensor(
                    h_sb[:, 0:fl], sg_sb[:, 0:fl], bank_f32(ub, fl), ALU.mult), s_dv)
                u_free[slot] = dv_h
                last_dvh = dv_h
                hq = hq_sb[slot]
                wait("vector", s_pe, tr0_free)
                dv_hq = quant(h_sb[:, 0:fl], hq[:, 0:fl], fl, sem=s_dv)
                wait("tensor", s_dv, dv_hq)
                if _DEBUG and t == 0 and si == 0 and dbg:
                    wait("gpsimd", s_ac, ac_s)
                    dma("gpsimd", dbg["dbg_sg"], sg_sb[:, 0:512], s_sto)
                    wait("gpsimd", s_dv, dv_hq)
                    dma("gpsimd", dbg["dbg_h"], h_sb[:, 0:512], s_sto)
                    dma("gpsimd", dbg["dbg_hq"], hq[:, 0:512], s_sto)
                nch = fl // P
                pe_t2 = None
                for c in range(nch):
                    pe_t2 = op("tensor", lambda te, c=c, hq=hq: te.transpose(
                        bank_bf16(0, P, off=c * P), hq[:, c * P:(c + 1) * P],
                        ident_bf[:]), s_pe if c == nch - 1 else None)
                tr0_free = pe_t2
                wait("vector", s_pe, pe_t2)
                ev2 = None
                for c in range(nch):
                    ic = f0 // P + c
                    ev2 = op("vector", lambda v, c=c, ic=ic, hqT=hqT: v.tensor_copy(
                        hqT[:, ic * P:(ic + 1) * P], bank_bf16(0, P, off=c * P)), s_dv)
                wait("tensor", s_dv, ev2)
                for dh in range(2):
                    zb = 6 + dh
                    if si == 0:
                        wait("tensor", s_dv, z_free[dh])
                    lastz = None
                    for c in range(nch):
                        ic = f0 // P + c
                        is_last = (si == len(FSL) - 1 and c == nch - 1)
                        lastz = op("tensor", lambda te, ic=ic, dh=dh, zb=zb, hqT=hqT,
                                   strt=(si == 0 and c == 0), stp=is_last: te.matmul(
                            bank_f32(zb, 512), hqT[:, ic * P:(ic + 1) * P],
                            wdT_sb[:, ic * D + dh * 512: ic * D + (dh + 1) * 512],
                            start=strt, stop=stp),
                            s_pe if is_last else None)
                    if si == len(FSL) - 1:
                        z_done[dh] = lastz
            x1b = x1t_sb[t % 2]
            wait("vector", s_sto, sto_hist[t % 2])
            dv_r = None
            for dh in range(2):
                wait("vector", s_pe, z_done[dh])
                dv_r = op("vector", lambda v, dh=dh, x1b=x1b, xb=xb: v.scalar_tensor_tensor(
                    x1b[:, dh * 512:(dh + 1) * 512], bank_f32(6 + dh, 512), 0.5,
                    xb[:, dh * 512:(dh + 1) * 512], ALU.mult, ALU.add), s_dv)
                z_free[dh] = dv_r
            resid[t % 2] = dv_r
            wait("sync", s_dv, dv_r)
            sto_hist[t % 2] = dma("sync", dst_t[t], x1b[:], s_sto)
        return dv_r

    def a1_pass(kind, w_hbm, bias_sb, prev_pe=None, prev_lns=(None, None),
                pre_sync=()):
        wait("gpsimd", s_pe, prev_pe)
        for (sem, val) in pre_sync:
            wait("sync", sem, val)
        wl = None
        for c in range(ND):
            wl = dma("gpsimd", wproj_sb[:, c * D:(c + 1) * D],
                     w_hbm.rearrange("(c p) e -> c p e", p=P)[c], s_ldw)
        src_t = x1_hbm.rearrange("(t p) d -> t p d", p=P)
        ln_done = list(prev_lns)
        tr_free = None
        proj_done = None
        prev_dv_e = None
        prev_st = [None, None]
        ev_done = None
        for g in range(8):
            xc = xlc_sb[g % 2]
            for tl in range(4):
                t = g * 4 + tl
                xb = xin_sb[t % 2]
                wait("sync", s_dv, ln_done[t % 2])
                lv = dma("sync", xb[:], src_t[t], s_ldx)
                wait("vector", s_ldx, lv)
                xq = xq_sb[t % 2]
                wait("vector", s_pe, tr_free)
                dv_a = op("vector", lambda v, t=t, xb=xb, xq=xq: v.tensor_scalar(
                    xq[:], xb[:], stat_m[:, t:t + 1], stat_r[:, t:t + 1],
                    ALU.subtract, ALU.mult), s_dv)
                ln_done[t % 2] = dv_a
                wait("tensor", s_dv, dv_a)
                if g == 0 and tl == 0:
                    wait("tensor", s_ldw, wl)
                if tl == 0:
                    wait("vector", s_pe, proj_done)
                pe_t = None
                for c in range(ND):
                    pe_t = op("tensor", lambda te, c=c, xq=xq: te.transpose(
                        bank_bf16(1, P, off=c * P), xq[:, c * P:(c + 1) * P],
                        ident_bf[:]), s_pe if c == ND - 1 else None)
                tr_free = pe_t
                wait("vector", s_pe, pe_t)
                for c in range(ND):
                    ev_done = op("vector", lambda v, c=c, xc=xc, tl=tl: v.tensor_copy(
                        xc[:, c * QT + tl * P: c * QT + (tl + 1) * P],
                        bank_bf16(1, P, off=c * P)), s_dv)
            wait("tensor", s_dv, ev_done)
            if _DEBUG and dbg and kind == "k" and g == 0:
                wait("gpsimd", s_dv, ev_done)
                wait("gpsimd", s_ldw, wl)
                dma("gpsimd", dbg["dbg_xlc"], xc[:], s_ldw)
                dma("gpsimd", dbg["dbg_wproj"], wproj_sb[:, 0:D], s_ldw)
            for ec in range(ND):
                buf = hq_sb[ec % 2]
                wait("tensor", s_dv, prev_dv_e)
                mm = None
                for c in range(ND):
                    mm = op("tensor", lambda te, c=c, ec=ec, xc=xc: te.matmul(
                        bank_f32(6, QT), wproj_sb[:, c * D + ec * P: c * D + (ec + 1) * P],
                        xc[:, c * QT:(c + 1) * QT],
                        start=(c == 0), stop=(c == ND - 1)),
                        s_pe if c == ND - 1 else None)
                proj_done = mm
                wait("vector", s_pe, mm)
                if kind == "q":
                    wait("vector", s_sto, prev_st[ec % 2])
                    dv_e = op("vector", lambda v, ec=ec, buf=buf: v.tensor_scalar(
                        buf[:, 0:QT], bank_f32(6, QT), bias_sb[:, ec:ec + 1],
                        0.25, ALU.add, ALU.mult), s_dv)
                    wait("sync", s_dv, dv_e)
                    prev_st[ec % 2] = dma(
                        "sync", q_hbm.rearrange("(c p) tt -> c p tt", p=P)[ec, :, g * QT:(g + 1) * QT],
                        buf[:, 0:QT], s_sto)
                elif kind == "k":
                    dv_e = op("vector", lambda v, ec=ec, g=g: v.tensor_scalar(
                        K_sb[:, ec * T + g * QT: ec * T + (g + 1) * QT],
                        bank_f32(6, QT), bias_sb[:, ec:ec + 1], None, ALU.add), s_dv)
                else:
                    dv_e = op("vector", lambda v, ec=ec, buf=buf: v.tensor_scalar(
                        buf[:, 0:QT], bank_f32(6, QT), bias_sb[:, ec:ec + 1],
                        None, ALU.add), s_dv)
                    wait("tensor", s_dv, dv_e)
                    pv_t = None
                    for j in range(4):
                        pv_t = op("tensor", lambda te, j=j, buf=buf: te.transpose(
                            bank_bf16(7, P, off=j * P), buf[:, j * P:(j + 1) * P],
                            ident_bf[:]), s_pe if j == 3 else None)
                    wait("vector", s_pe, pv_t)
                    lastv = None
                    for j in range(4):
                        kc = g * 4 + j
                        for hh in range(2):
                            h = ec * 2 + hh
                            lastv = op("vector", lambda v, j=j, kc=kc, h=h, hh=hh: v.tensor_copy(
                                V2T_sb[:, kc * (H * 65) + h * 65: kc * (H * 65) + h * 65 + 64],
                                bank_bf16(7, 64, off=j * P + hh * 64)), s_dv)
                    dv_e = lastv
                prev_dv_e = dv_e
        return proj_done, prev_dv_e, ln_done, s_sto.n

    def kmax_calc(pre_sto=None):
        wait("vector", s_sto, pre_sto)
        mm_hist = [None, None]
        f2 = None
        for nchk in range(8):
            last_mm = None
            for ec in range(ND):
                buf = hq_sb[ec % 2]
                wait("vector", s_pe, mm_hist[ec % 2])
                dv_s = op("vector", lambda v, ec=ec, nchk=nchk, buf=buf: v.scalar_tensor_tensor(
                    buf[:, 0:QT],
                    K_sb[:, ec * T + nchk * QT: ec * T + (nchk + 1) * QT], 0.0,
                    K_sb[:, ec * T + nchk * QT: ec * T + (nchk + 1) * QT],
                    ALU.add, ALU.mult), s_dv)
                wait("tensor", s_dv, dv_s)
                last_mm = op("tensor", lambda te, ec=ec, buf=buf: te.matmul(
                    bank_f32(6, QT, rows=H), ind_sb[:, ec * H:(ec + 1) * H],
                    buf[:, 0:QT], start=(ec == 0), stop=(ec == ND - 1)), s_pe)
                mm_hist[ec % 2] = last_mm
            wait("vector", s_pe, last_mm)
            f1 = op("vector", lambda v: v.reduce_max(
                tmax_sb[0:H, 0:1], bank_f32(6, QT, rows=H), axis=AX.X), s_dv)
            wait("vector", s_dv, f1)
            if nchk == 0:
                f2 = op("vector", lambda v: v.tensor_copy(
                    kmax_sb[0:H, 0:1], tmax_sb[0:H, 0:1]), s_dv)
            else:
                f2 = op("vector", lambda v: v.tensor_tensor(
                    kmax_sb[0:H, 0:1], kmax_sb[0:H, 0:1], tmax_sb[0:H, 0:1],
                    ALU.max), s_dv)
            wait("vector", s_dv, f2)
        return f2

    def a2_phase(kmax_done, nqt=NQT, do_heads=True, do_norm=True, do_out=True, do_qn=True, do_negc=True, do_wo=True, do_bo=True, prev_pe=None, prev_lns=(None, None)):
        wait("gpsimd", s_pe, prev_pe)
        wol = None
        for c in (range(ND) if do_wo else []):
            wol = dma("gpsimd", woT_sb[:, c * D:(c + 1) * D],
                      wT_in["woT"].rearrange("(c p) e -> c p e", p=P)[c], s_ldw)
        bo_ev = None
        if not do_bo:
            return None
        bol = dma("gpsimd", Pch_sb[0][0:1, 0:D], bo_in, s_ldw)
        wait("tensor", s_ldw, bol)
        wait("tensor", s_dv, kmax_done)
        for half in range(2):
            bom = op("tensor", lambda te, half=half: te.matmul(
                bank_f32(6, 512), ones_bf[0:1, :],
                Pch_sb[0][0:1, half * 512:(half + 1) * 512], start=True, stop=True), s_pe)
            wait("vector", s_pe, bom)
            bo_ev = op("vector", lambda v, half=half: v.tensor_copy(
                bo_bc[:, half * 512:(half + 1) * 512], bank_f32(6, 512)), s_dv)
            wait("tensor", s_dv, bo_ev)

        exp_hist = {}
        norm_done = [None, None]
        x2_res = list(prev_lns)
        x2_sto = [None, None]
        ostg_dma = None
        last_out_mm = None
        last_d1 = None
        for qt in range(nqt):
            ql = None
            for ec in range(ND):
                ql = dma("sync", Qt_sb[:, ec * QT:(ec + 1) * QT],
                         q_hbm.rearrange("(c p) tt -> c p tt", p=P)[ec, :, qt * QT:(qt + 1) * QT],
                         s_ldx)
            qn_hist = [None, None]
            qn_mm = None
            for ec in (range(ND) if do_qn else []):
                buf = hq_sb[ec % 2]
                wait("vector", s_ldx, ql)
                wait("vector", s_pe, qn_hist[ec % 2])
                dv_s = op("vector", lambda v, ec=ec, buf=buf: v.scalar_tensor_tensor(
                    buf[:, 0:QT], Qt_sb[:, ec * QT:(ec + 1) * QT], 0.0,
                    Qt_sb[:, ec * QT:(ec + 1) * QT], ALU.add, ALU.mult), s_dv)
                wait("tensor", s_dv, dv_s)
                if ec == 0:
                    wait("tensor", s_dv, norm_done[0])
                    wait("tensor", s_dv, norm_done[1])
                    wait("tensor", s_dv, bo_ev)
                qn_mm = op("tensor", lambda te, ec=ec, buf=buf: te.matmul(
                    bank_f32(6, QT, rows=H), ind_sb[:, ec * H:(ec + 1) * H],
                    buf[:, 0:QT], start=(ec == 0), stop=(ec == ND - 1)), s_pe)
                qn_hist[ec % 2] = qn_mm
            wait("vector", s_pe, qn_mm)
            if qt == 0:
                wait("vector", s_dv, kmax_done)
            if not do_negc:
                continue
            g1 = op("vector", lambda v: v.reduce_max(
                qmax_sb[0:H, 0:1], bank_f32(6, QT, rows=H), axis=AX.X), s_dv)
            wait("vector", s_dv, g1)
            g2 = op("vector", lambda v: v.tensor_tensor(
                tmax_sb[0:H, 1:2], qmax_sb[0:H, 0:1], kmax_sb[0:H, 0:1], ALU.mult), s_dv)
            wait("scalar", s_dv, g2)
            g3 = op("scalar", lambda s: s.activation(
                tmax_sb[0:H, 2:3], tmax_sb[0:H, 1:2], AF.Ln), s_ac)
            wait("scalar", s_ac, g3)
            g4 = op("scalar", lambda s: s.activation(
                tmax_sb[0:H, 3:4], tmax_sb[0:H, 2:3], AF.Exp, scale=0.5), s_ac)
            wait("vector", s_ac, g4)
            g5 = op("vector", lambda v: v.tensor_scalar(
                rrow_sb[0:H, 64:65], tmax_sb[0:H, 3:4], -1.0, None, ALU.mult), s_dv)
            wait("tensor", s_dv, g5)
            g6 = op("tensor", lambda te: te.transpose(
                bank_bf16(6, H, rows=1), rrow_sb[0:H, 64:65], ident_bf[0:H, 0:H]), s_pe)
            wait("vector", s_pe, g6)
            g7 = op("vector", lambda v: v.tensor_copy(
                rrow_sb[0:1, 96:96 + H], bank_bf16(6, H, rows=1)), s_dv)
            wait("tensor", s_dv, g7)
            g8 = op("tensor", lambda te: te.matmul(
                bank_f32(6, H), ones_bf[0:1, :], rrow_sb[0:1, 96:96 + H],
                start=True, stop=True), s_pe)
            wait("vector", s_pe, g8)
            g9 = op("vector", lambda v: v.tensor_copy(
                negc_sb[:, 0:H], bank_f32(6, H)), s_dv)
            wait("scalar", s_dv, g9)
            if _DEBUG and qt == 0 and dbg:
                wait("gpsimd", s_dv, g9)
                dma("gpsimd", dbg["dbg_K"], K_sb[:, 0:512], s_ldw)
                dma("gpsimd", dbg["dbg_V"], V2T_sb[:, 0:1040], s_ldw)
                dma("gpsimd", dbg["dbg_Qt"], Qt_sb[:], s_ldw)
                dma("gpsimd", dbg["dbg_negc"], negc_sb[:, 0:H], s_ldw)
                dma("gpsimd", dbg["dbg_kmax"], kmax_sb[:, 0:4], s_ldw)
                dma("gpsimd", dbg["dbg_K7"], K_sb[:, 0 * T + 3584:0 * T + 4096], s_ldw)
                dma("gpsimd", dbg["dbg_V7"], V2T_sb[:, 28 * (H * 65):28 * (H * 65) + 1040], s_ldw)
            wait("vector", s_ldw, ostg_dma)
            wait("vector", s_pe, last_out_mm)
            for h in (range(H) if do_heads else []):
                ec, hh = h // 2, h % 2
                ob = 4 + (h % 2)
                wait("tensor", s_dv, norm_done[h % 2])
                if h == 0:
                    wait("tensor", s_dv, g9)
                for i in range(16):
                    par = i % 2
                    sbk = par * 2
                    wait("tensor", s_ac, exp_hist.get((qt, h, i - 2)))
                    if i == 0:
                        wait("tensor", s_ac, exp_hist.get((qt, h - 1, 15)) or
                             exp_hist.get((qt - 1, H - 1, 15)))
                    s_mm = None
                    for u in range(2):
                        kc = i * 2 + u
                        s_mm = op("tensor", lambda te, kc=kc, u=u, ec=ec, hh=hh, sbk=sbk: te.matmul(
                            PS[0:P, (sbk + u) * 512:(sbk + u + 1) * 512],
                            K_sb[hh * 64:(hh + 1) * 64, ec * T + kc * P: ec * T + (kc + 1) * P],
                            Qt_sb[hh * 64:(hh + 1) * 64, ec * QT:(ec + 1) * QT],
                            start=True, stop=True), s_pe if u == 1 else None)
                    if i > 0:
                        ip = i - 1
                        parp = ip % 2
                        wait("tensor", s_ac, exp_hist[(qt, h, ip)])
                        for u in range(2):
                            kc = ip * 2 + u
                            op("tensor", lambda te, kc=kc, u=u, h=h, parp=parp, ob=ob, ip=ip: te.matmul(
                                PS[0:65, ob * 512:(ob + 1) * 512],
                                V2T_sb[:, kc * (H * 65) + h * 65: kc * (H * 65) + (h + 1) * 65],
                                Pch_sb[parp][:, u * 512:(u + 1) * 512],
                                start=(ip == 0 and u == 0), stop=False))
                    wait("scalar", s_pe, s_mm)
                    exp_hist[(qt, h, i)] = op(
                        "scalar", lambda s, par=par, sbk=sbk, h=h: s.activation(
                            Pch_sb[par][:, 0:1024], PS[0:P, sbk * 512:(sbk + 2) * 512],
                            AF.Exp, bias=negc_sb[:, h:h + 1]), s_ac)
                wait("tensor", s_ac, exp_hist[(qt, h, 15)])
                pv = None
                for u in range(2):
                    kc = 30 + u
                    pv = op("tensor", lambda te, kc=kc, u=u, h=h, ob=ob: te.matmul(
                        PS[0:65, ob * 512:(ob + 1) * 512],
                        V2T_sb[:, kc * (H * 65) + h * 65: kc * (H * 65) + (h + 1) * 65],
                        Pch_sb[1][:, u * 512:(u + 1) * 512],
                        start=False, stop=(u == 1)), s_pe if u == 1 else None)
                wait("vector", s_pe, pv)
                if not do_norm:
                    norm_done[h % 2] = op("vector", lambda v, ob=ob, ec=ec, hh=hh: v.tensor_copy(
                        (OallT_sb if hh == 0 else Ostg_sb)[0:64, ec * QT:(ec + 1) * QT],
                        PS[0:64, ob * 512:(ob + 1) * 512]), s_dv)
                    continue
                n1 = op("vector", lambda v, ob=ob: v.tensor_single_scalar(
                    sums_sb[64:65, 0:QT], PS[64:65, ob * 512:(ob + 1) * 512],
                    1e-30, ALU.max), s_dv)
                wait("tensor", s_dv, n1)
                n2 = None
                for j in range(4):
                    n2 = op("tensor", lambda te, j=j: te.transpose(
                        PS[0:P, 6 * 512 + 32 + j:6 * 512 + 32 + j + 1],
                        sums_sb[64:65, j * P:(j + 1) * P], ident_f32[64:65, 64:65]),
                        s_pe if j == 3 else None)
                wait("vector", s_pe, n2)
                n3 = op("vector", lambda v: v.tensor_copy(
                    srt_sb[:, 0:4], PS[0:P, 6 * 512 + 32:6 * 512 + 36]), s_dv)
                wait("vector", s_dv, n3)
                n4 = op("vector", lambda v: v.reciprocal(srt_sb[:, 4:8], srt_sb[:, 0:4]), s_dv)
                wait("tensor", s_dv, n4)
                n5 = None
                for j in range(4):
                    n5 = op("tensor", lambda te, j=j: te.transpose(
                        PS[0:1, 6 * 512 + j * P: 6 * 512 + (j + 1) * P],
                        srt_sb[:, 4 + j:5 + j], ident_f32[:]), s_pe if j == 3 else None)
                wait("vector", s_pe, n5)
                n6 = op("vector", lambda v: v.tensor_copy(
                    rrow_sb[0:1, 0:QT], PS[0:1, 6 * 512:6 * 512 + QT]), s_dv)
                wait("tensor", s_dv, n6)
                n7 = op("tensor", lambda te: te.matmul(
                    PS[0:64, 7 * 512:7 * 512 + QT], ones_bf[0:1, 0:64],
                    rrow_sb[0:1, 0:QT], start=True, stop=True), s_pe)
                wait("vector", s_pe, n7)
                n8 = op("vector", lambda v: v.tensor_copy(
                    rb_sb[0:64, 0:QT], PS[0:64, 7 * 512:7 * 512 + QT]), s_dv)
                wait("vector", s_dv, n8)
                dstv = OallT_sb if hh == 0 else Ostg_sb
                n9 = op("vector", lambda v, ob=ob, dstv=dstv, ec=ec: v.tensor_tensor(
                    dstv[0:64, ec * QT:(ec + 1) * QT],
                    PS[0:64, ob * 512:(ob + 1) * 512],
                    rb_sb[0:64, 0:QT], ALU.mult), s_dv)
                norm_done[h % 2] = n9
            if not do_out:
                continue
            wait("gpsimd", s_dv, norm_done[0])
            wait("gpsimd", s_dv, norm_done[1])
            ostg_dma = dma("gpsimd", OallT_sb[64:128, :], Ostg_sb[0:64, :], s_ldw)
            if _DEBUG and qt == 0 and dbg:
                wait("gpsimd", s_ldw, ostg_dma)
                dma("gpsimd", dbg["dbg_O"], OallT_sb[:], s_ldw)
            wait("tensor", s_ldw, ostg_dma)
            wait("tensor", s_ldw, wol)
            wait("tensor", s_dv, norm_done[0])
            wait("tensor", s_dv, norm_done[1])
            for qs in range(4):
                xr = xin_sb[qs % 2]
                wait("sync", s_dv, x2_res[qs % 2])
                xl = dma("sync", xr[:],
                         x1_hbm.rearrange("(t p) d -> t p d", p=P)[qt * 4 + qs], s_ldx)
                for dh in range(2):
                    wait("tensor", s_dv, last_d1)
                    mm = None
                    for ec in range(ND):
                        mm = op("tensor", lambda te, ec=ec, qs=qs, dh=dh: te.matmul(
                            PS[0:P, 7 * 512:8 * 512],
                            OallT_sb[:, ec * QT + qs * P: ec * QT + (qs + 1) * P],
                            woT_sb[:, ec * D + dh * 512: ec * D + (dh + 1) * 512],
                            start=(ec == 0), stop=(ec == ND - 1)),
                            s_pe if ec == ND - 1 else None)
                    last_out_mm = mm
                    wait("vector", s_pe, mm)
                    wait("vector", s_ldx, xl)
                    if dh == 0:
                        wait("vector", s_sto, x2_sto[qs % 2])
                    d1 = op("vector", lambda v, dh=dh, xr=xr, qs=qs: v.tensor_tensor(
                        x1t_sb[qs % 2][:, dh * 512:(dh + 1) * 512],
                        PS[0:P, 7 * 512:8 * 512],
                        xr[:, dh * 512:(dh + 1) * 512], ALU.add), s_dv)
                    last_d1 = d1
                    wait("vector", s_dv, d1)
                    d2 = op("vector", lambda v, dh=dh, qs=qs: v.tensor_tensor(
                        x1t_sb[qs % 2][:, dh * 512:(dh + 1) * 512],
                        x1t_sb[qs % 2][:, dh * 512:(dh + 1) * 512],
                        bo_bc[:, dh * 512:(dh + 1) * 512], ALU.add), s_dv)
                    x2_res[qs % 2] = d2
                wait("sync", s_dv, x2_res[qs % 2])
                x2_sto[qs % 2] = dma(
                    "sync", x2_hbm.rearrange("(t p) d -> t p d", p=P)[qt * 4 + qs],
                    x1t_sb[qs % 2][:], s_sto)
        return last_out_mm

    dma("gpsimd", ident_bf[:], ident_in, s_ldw)
    dma("gpsimd", ones_bf[:], ones_in, s_ldw)
    cst = dma("gpsimd", ind_sb[:].rearrange("p (c h) -> p c h", h=H),
              ind_in.rearrange("(c p) h -> p c h", p=P), s_ldw)
    lb = dma("sync", ident_f32[:], ident_in, s_ldx)
    for n in ("bq_r", "bk_r", "bv_r"):
        lb = dma("sync", b_sb[n][:], b_in[n], s_ldx)
    const_gp = cst
    wait("vector", s_ldx, lb)

    f1a_done, f1a_ac, f1a_dv = ln_stats_phase(x_in, stat_m, stat_r)
    wq_last = weight_prep(pre_wait_vec=f1a_done)
    f1_dst = out_hbm if _STAGE == "f1" else x1_hbm
    ffn_phase(x_in, stat_m, stat_r, f1_dst,
              w_ready_dv=wq_last, w_ready_gp=const_gp,
              pre_wait=(("sync", s_ac, f1a_done), ("vector", s_ac, f1a_done)))

    if _STAGE != "f1":
        f1_pe_end = s_pe.n
        f1_dv_end = s_dv.n
        a_done, _, a_dv_last = ln_stats_phase(
            x1_hbm, stat_m, stat_r, pre_wait_sync=[(s_sto, s_sto.n)])
        wait("vector", s_ac, a_done)
        wait("vector", s_pe, f1_pe_end)
        wait("gpsimd", s_dv, f1_dv_end)
        wait("gpsimd", s_pe, f1_pe_end)
        v2ones = V2T_sb[:].rearrange("p (kc h j) -> p kc h j", h=H, j=65)[:, :, :, 64:65]
        on("vector", lambda v: v.memset(v2ones, 1.0))
        pv_done, _, v_lns, _ = a1_pass(
            "v", wT_in["wvT"], b_sb["bv_r"],
            pre_sync=[(s_ac, a_done), (s_dv, a_dv_last)])
        if _STAGE != "a1v":
            pk_done, _, k_lns, _ = a1_pass("k", wT_in["wkT"], b_sb["bk_r"],
                                           prev_pe=pv_done, prev_lns=v_lns)
            pq_done, _, q_lns, q_sto = a1_pass("q", wT_in["wqT"], b_sb["bq_r"],
                                               prev_pe=pk_done, prev_lns=k_lns)
            km = kmax_calc(pre_sto=q_sto)
        if _STAGE not in ("a1v", "a1"):
            a2_last = a2_phase(km, prev_pe=pq_done, prev_lns=q_lns, nqt=(0 if _STAGE in ("att1q", "att1w", "att1x") else 1 if _STAGE.startswith("att1") else NQT),
                               do_heads=(_STAGE != "att1a"),
                               do_norm=(_STAGE not in ("att1a", "att1b")),
                               do_out=(_STAGE not in ("att1a", "att1b", "att1c")),
                               do_qn=(_STAGE not in ("att1p", "att1q")),
                               do_negc=(_STAGE not in ("att1p", "att1q", "att1r")),
                               do_wo=(_STAGE != "att1w"),
                               do_bo=(_STAGE not in ("att1w", "att1x")))
        if _STAGE in ("a1v", "a1") or _STAGE.startswith("att1"):
            wait("sync", s_sto, s_sto.n)
            wait("sync", s_dv, s_dv.n)
            wait("sync", s_pe, s_pe.n)
            dma("sync", out_hbm, x1_hbm, s_sto)
        elif _STAGE == "att":
            wait("sync", s_sto, s_sto.n)
            n = dma("sync", out_hbm, x2_hbm, s_sto)
        else:
            f2a_done, _, _ = ln_stats_phase(
                x2_hbm, stat_m, stat_r, pre_wait_sync=[(s_sto, s_sto.n)])
            wait("gpsimd", s_pe, a2_last)
            ffn_phase(x2_hbm, stat_m, stat_r, out_hbm,
                      w_ready_dv=None, w_ready_gp=None, load_weights=True,
                      pre_wait=(("sync", s_ac, f2a_done),
                                ("vector", s_ac, f2a_done),
                                ("vector", s_pe, a2_last)))

    wait("sync", s_sto, s_sto.n)
    on("sync", lambda e: e.nop())

    block = st.enter_context(nc.Block())
    for name, deco in (("sync", block.sync), ("gpsimd", block.gpsimd),
                       ("tensor", block.tensor), ("vector", block.vector),
                       ("scalar", block.scalar)):
        def make(name):
            def run(e):
                for f in streams[name]:
                    f(e)
            return run
        deco(make(name))

    st.close()
    return nc


def _host_prep(inputs):
    x = np.asarray(inputs["L0_attn_x"], dtype=np.float32)
    ind = np.zeros((ND * P, H), np.float32)
    for ec in range(ND):
        for r in range(P):
            ind[ec * P + r, ec * 2 + r // 64] = 1.0
    common = {
        "wg": np.ascontiguousarray(np.asarray(inputs["L0_ffn_w_gate"], np.float32)),
        "wu": np.ascontiguousarray(np.asarray(inputs["L0_ffn_w_up"], np.float32)),
        "wd": np.ascontiguousarray(np.asarray(inputs["L0_ffn_w_down"], np.float32)),
        "wqT": np.ascontiguousarray(np.asarray(inputs["L0_attn_wq"], np.float32).T),
        "wkT": np.ascontiguousarray(np.asarray(inputs["L0_attn_wk"], np.float32).T),
        "wvT": np.ascontiguousarray(np.asarray(inputs["L0_attn_wv"], np.float32).T),
        "woT": np.ascontiguousarray(np.asarray(inputs["L0_attn_wo"], np.float32).T),
        "bq_r": np.ascontiguousarray(np.asarray(inputs["L0_attn_bq"], np.float32).reshape(ND, P).T),
        "bk_r": np.ascontiguousarray(np.asarray(inputs["L0_attn_bk"], np.float32).reshape(ND, P).T),
        "bv_r": np.ascontiguousarray(np.asarray(inputs["L0_attn_bv"], np.float32).reshape(ND, P).T),
        "bo": np.ascontiguousarray(np.asarray(inputs["L0_attn_bo"], np.float32).reshape(1, D)),
        "ident": np.eye(P, dtype=np.float32),
        "onesr": np.ones((1, P), np.float32),
        "ind": ind,
    }
    return [dict(common, x=np.ascontiguousarray(x[c])) for c in range(x.shape[0])]


def kernel(**inputs):
    in_maps = _host_prep(inputs)
    nc = build_nc()
    res = run_bass_kernel_spmd(nc, in_maps, core_ids=list(range(len(in_maps))))
    return np.stack([r["out"] for r in res.results], axis=0)
